# version 01
# baseline (speedup 1.0000x reference)
import numpy as np
import ml_dtypes
import concourse.bass as bass
import concourse.mybir as mybir
from concourse.bass_utils import run_bass_kernel_spmd

F32 = mybir.dt.float32
BF16 = mybir.dt.bfloat16
AF = mybir.ActivationFunctionType
ALU = mybir.AluOpType
AX = mybir.AxisListType

STOP_AFTER = 7
DEBUG_TAP = False
LAST_RES = None

T = 2048
NTT = 16
D = 1024
DC = 8
NE = 24
FF = 4096
NH = 16
HD = 64
MAGIC = 12582912.0
LN_EPS = 1e-5

OWNER = {"LD": "sync", "WG": "gpsimd", "TE": "tensor", "VE": "vector", "SE": "scalar"}


def build_nc():
    nc = bass.Bass("TRN2", target_bir_lowering=False)
    items = []
    C = {"LD": 0, "WG": 0, "TE": 0, "VE": 0, "SE": 0}

    def em(eng, emit, inc=None, k=1, waits=()):
        ws = tuple((s, v) for (s, v) in waits if v and v > 0)
        items.append((eng, ws, emit, inc, k))
        if inc is not None:
            C[inc] += k
            return C[inc]
        return None

    def din(name, shape, dt=F32):
        return nc.dram_tensor(name, list(shape), dt, kind="ExternalInput").ap()

    x_own = din("x_own", [T, D])
    x_oth = din("x_oth", [T, D])
    wqT = din("wqT", [D, D]); wkT = din("wkT", [D, D])
    wvT = din("wvT", [D, D]); woT = din("woT", [D, D])
    gate_w = din("gate_w", [D, NE])
    moe_w1 = din("moe_w1", [NE, D, FF])
    moe_w2 = din("moe_w2", [NE, FF, D])
    moe_b1p = din("moe_b1p", [128, NE * 32])
    moe_b2 = din("moe_b2", [NE, D])
    qkvW = din("qkvW", [D, 3 * D])
    woutW = din("woutW", [D, D])
    l1w1T = din("l1w1T", [D, FF])
    l1w2T = din("l1w2T", [FF, D])
    biasA_in = din("biasA_in", [128, 1024])
    biasB_in = din("biasB_in", [128, 1024])
    reps_in = din("reps_in", [128, 15 * 1024])
    idn_in = din("idn_in", [128, 128])
    iota_in = din("iota_in", [128, NE])
    iotam_in = din("iotam_in", [128, NE])

    y = nc.dram_tensor("y", [T, D], F32, kind="ExternalOutput").ap()
    if DEBUG_TAP:
        dbg_xn = nc.dram_tensor("dbg_xn", [128, 16384], BF16, kind="ExternalOutput").ap()
        dbg_t3 = nc.dram_tensor("dbg_t3", [128, 16384], BF16, kind="ExternalOutput").ap()
        dbg_t5 = nc.dram_tensor("dbg_t5", [128, 16384], BF16, kind="ExternalOutput").ap()
        dbg_t4 = nc.dram_tensor("dbg_t4", [128, 16384], BF16, kind="ExternalOutput").ap()
        dbg_st = nc.dram_tensor("dbg_st", [128, 256], F32, kind="ExternalOutput").ap()
        dbg_y1 = nc.dram_tensor("dbg_y1", [128, 1024], F32, kind="ExternalOutput").ap()
        dbg_qx = nc.dram_tensor("dbg_qx", [128, 1024], BF16, kind="ExternalOutput").ap()
        dbg_h = nc.dram_tensor("dbg_h", [128, 4096], BF16, kind="ExternalOutput").ap()
        dbg_hg = nc.dram_tensor("dbg_hg", [128, 4096], BF16, kind="ExternalOutput").ap()
        dbg_qh = nc.dram_tensor("dbg_qh", [128, 4096], BF16, kind="ExternalOutput").ap()
        dbg_x01 = nc.dram_tensor("dbg_x01", [128, 2048], F32, kind="ExternalOutput").ap()
    wq_kind = "ExternalOutput" if DEBUG_TAP else "Internal"
    w1q_hbm = nc.dram_tensor("w1q_hbm", [128, DC * FF], BF16, kind=wq_kind).ap()
    w2q_hbm = nc.dram_tensor("w2q_hbm", [128, 32 * D], BF16, kind=wq_kind).ap()

    mgr = []

    def alloc(kind, name, shape, dt):
        m = (nc.sbuf_tensor if kind == "s" else nc.psum_tensor)(name, list(shape), dt)
        mgr.append(m)
        return m.__enter__()

    X = alloc("s", "X", [128, NTT * 1024], F32)
    XN = alloc("s", "XN", [128, DC * 2048], BF16)
    T3 = alloc("s", "T3", [128, 16384], BF16)
    T4 = alloc("s", "T4", [128, 16384], BF16)
    T5 = alloc("s", "T5", [128, 16384], BF16)
    T9a = alloc("s", "T9a", [128, 1408], F32)
    T9c = alloc("s", "T9c", [128, 2048], BF16)
    BIASA = alloc("s", "BIASA", [128, 1024], BF16)
    BIASB = alloc("s", "BIASB", [128, 1024], BF16)
    IDN = alloc("s", "IDN", [128, 128], BF16)
    IOTA = alloc("s", "IOTA", [128, NE], F32)
    IOTAM = alloc("s", "IOTAM", [128, NE], F32)
    ONES = alloc("s", "ONES", [128, 128], BF16)
    ONESF = alloc("s", "ONESF", [1, 128], F32)
    STAT = alloc("s", "STAT", [128, 256], F32)
    FLUT = alloc("s", "FLUT", [128, 8], F32)

    PS_S = alloc("p", "PS_S", [128, 2048], F32)
    PS_T = alloc("p", "PS_T", [128, 1024], BF16)
    PS_U = alloc("p", "PS_U", [128, 1024], BF16)
    PS_C = alloc("p", "PS_C", [128, 512], F32)
    PS_E = alloc("p", "PS_E", [128, 512], F32)

    sems = {}
    for sname in ["LD", "WG", "TE", "VE", "SE"]:
        m = nc.semaphore(sname)
        mgr.append(m)
        sems[sname] = m.__enter__()

    ST_M, ST_R = 0, 16
    ST_A = 32
    ST_B = 48
    ST_W = 64
    ST_Q = 80
    ST_BN = 96
    ST_G = 150
    A_STAGE, A_COMB = 0, 1024

    FLU2 = FLUT[:, 0:1]
    FLUS = FLUT[:, 4:5]

    def dma(eng_name, out, in_, waits=()):
        sem = "LD" if eng_name == "sync" else "WG"
        return em(eng_name, lambda e, o=out, i=in_: e.dma_start(o, i),
                  inc=sem, k=16, waits=waits)

    def t_op(emit, waits=()):
        return em("tensor", emit, inc="TE", waits=waits)

    FLU = [None]

    def v_op(emit, waits=(), flush=True):
        tok = em("vector", emit, inc="VE", waits=waits)
        if flush:
            tok = em("vector", lambda e: e.tensor_copy(FLU2, FLUT[:, 2:3]), inc="VE")
        return tok

    def s_op(emit, waits=(), flush=True):
        tok = em("scalar", emit, inc="SE", waits=waits)
        if flush:
            tok = em("scalar", lambda e: e.copy(FLUS, FLUS), inc="SE")
        return tok

    def mm(out, lhsT, rhs, start, stop, waits=()):
        return t_op(lambda e, o=out, l=lhsT, r=rhs, st=start, sp=stop:
                    e.matmul(o, l, r, start=st, stop=sp), waits=waits)

    def transpose(out, in_, waits=()):
        return t_op(lambda e, o=out, i=in_: e.transpose(o, i, IDN[:, :]),
                    waits=waits)

    def xn_view(buf, tt):
        return buf[:, :].rearrange("p (c t) -> p c t", t=2048)[:, :, tt * 128:(tt + 1) * 128]

    def ln_stats(src_ap, n, mcol, rcol, junk_ap, waits=()):
        sm = STAT[:, ST_BN:ST_BN + 1]
        sq = STAT[:, ST_BN + 1:ST_BN + 2]
        msq = STAT[:, ST_BN + 2:ST_BN + 3]
        v_op(lambda e, o=sm, s=src_ap:
             e.tensor_reduce(o, s, AX.X, ALU.add), waits=waits)
        tse = s_op(lambda e, o=junk_ap, s=src_ap, a=sq:
                   e.activation(o, s, AF.Square, accum_out=a), waits=waits)
        v_op(lambda e, o=STAT[:, mcol:mcol + 1], i=sm:
             e.tensor_single_scalar(o, i, 1.0 / n, ALU.mult))
        v_op(lambda e, o=msq, i=STAT[:, mcol:mcol + 1], m=STAT[:, mcol:mcol + 1]:
             e.tensor_tensor(o, i, m, ALU.mult))
        v_op(lambda e, o=STAT[:, rcol:rcol + 1], i=sq:
             e.tensor_single_scalar(o, i, 1.0 / n, ALU.mult), waits=(("SE", tse),))
        v_op(lambda e, o=STAT[:, rcol:rcol + 1], i=STAT[:, rcol:rcol + 1], m=msq:
             e.tensor_tensor(o, i, m, ALU.subtract))
        tve = v_op(lambda e, o=STAT[:, rcol:rcol + 1], i=STAT[:, rcol:rcol + 1]:
                   e.tensor_single_scalar(o, i, LN_EPS, ALU.add))
        tse = s_op(lambda e, o=STAT[:, rcol:rcol + 1]:
                   e.activation(o, o, AF.Sqrt), waits=(("VE", tve),))
        return v_op(lambda e, o=STAT[:, rcol:rcol + 1]:
                    e.reciprocal(o, o), waits=(("SE", tse),))

    tok_x = dma("sync", X[:, :], x_own.rearrange("(tt p) d -> p tt d", p=128))
    dma("gpsimd", BIASA[:, :], biasA_in)
    dma("gpsimd", BIASB[:, :], biasB_in)
    idn_tok = dma("gpsimd", IDN[:, :], idn_in)
    dma("sync", IOTA[:, :], iota_in)
    tok_iota = dma("sync", IOTAM[:, :], iotam_in)
    v_op(lambda e: e.memset(ONES[:, :], 1.0))
    v_op(lambda e: e.memset(ONESF[:, :], 1.0))

    def p0():
        for tt in range(NTT):
            xsl = X[:, tt * 1024:(tt + 1) * 1024]
            ln_stats(xsl, 1024, ST_M + tt, ST_R + tt, T9c[:, 1024:2048], waits=(("LD", tok_x),))
            nm = STAT[:, ST_B:ST_B + 1]
            s1 = STAT[:, ST_B + 1:ST_B + 2]
            s2 = STAT[:, ST_B + 2:ST_B + 3]
            mcol = STAT[:, ST_M + tt:ST_M + tt + 1]
            rcol = STAT[:, ST_R + tt:ST_R + tt + 1]
            v_op(lambda e, o=nm, i=mcol: e.tensor_single_scalar(o, i, -1.0, ALU.mult))
            v_op(lambda e, o=s1, i=rcol: e.tensor_single_scalar(o, i, 1.0, ALU.add))
            v_op(lambda e, o=s2, i=nm, r=rcol: e.tensor_scalar(o, i, r, None, ALU.mult))
            xt = T9c[:, 0:1024]
            tve = v_op(lambda e, o=xt, i=xsl, a=nm, b=rcol:
                       e.tensor_scalar(o, i, a, b, ALU.add, ALU.mult))
            v_op(lambda e, o=xsl, a=s1, b=s2:
                 e.tensor_scalar(o, o, a, b, ALU.mult, ALU.add))
            tte = None
            for c in range(DC):
                tte = transpose(PS_T[:, c * 128:(c + 1) * 128],
                                xt[:, c * 128:(c + 1) * 128],
                                waits=(("VE", tve), ("WG", idn_tok)))
            v_op(lambda e, o=xn_view(XN, tt),
                 i=PS_T[:, :].rearrange("p (c t) -> p c t", c=DC):
                 e.tensor_copy(o, i), waits=(("TE", tte),))
        for tt in range(NTT):
            stg = T9a[:, A_STAGE:A_STAGE + 1024]
            tok = dma("sync", stg,
                      x_oth.rearrange("(tt p) d -> p tt d", p=128)[:, tt, :],
                      waits=(("VE", C["VE"]),))
            ln_stats(stg, 1024, ST_M, ST_R, T9c[:, 1024:2048], waits=(("LD", tok),))
            nm = STAT[:, ST_B:ST_B + 1]
            mcol = STAT[:, ST_M:ST_M + 1]
            rcol = STAT[:, ST_R:ST_R + 1]
            v_op(lambda e, o=nm, i=mcol: e.tensor_single_scalar(o, i, -1.0, ALU.mult))
            xt = T9c[:, 1024:2048]
            tve = v_op(lambda e, o=xt, i=stg, a=nm, b=rcol:
                       e.tensor_scalar(o, i, a, b, ALU.add, ALU.mult))
            tte = None
            for c in range(DC):
                tte = transpose(PS_U[:, c * 128:(c + 1) * 128],
                                xt[:, c * 128:(c + 1) * 128], waits=(("VE", tve),))
            v_op(lambda e, o=xn_view(T3, tt),
                 i=PS_U[:, :].rearrange("p (c t) -> p c t", c=DC):
                 e.tensor_copy(o, i), waits=(("TE", tte),))

    def p1():
        KP0, VP0, E0, ET0 = 0, 4096, 8192, 12288
        QP0, CX0, WQ0, WK0, WV0, WO0 = 0, 2048, 4096, 5120, 6144, 7168
        for j in range(8):
            gate = C["TE"]
            twq = dma("gpsimd", T5[:, WQ0:WQ0 + 1024],
                      wqT.rearrange("(c p) i -> p c i", p=128)[:, :, j * 128:(j + 1) * 128],
                      waits=(("TE", gate),))
            twk = dma("gpsimd", T5[:, WK0:WK0 + 1024],
                      wkT.rearrange("(c p) i -> p c i", p=128)[:, :, j * 128:(j + 1) * 128])
            twv = dma("gpsimd", T5[:, WV0:WV0 + 1024],
                      wvT.rearrange("(c p) i -> p c i", p=128)[:, :, j * 128:(j + 1) * 128])
            two = dma("gpsimd", T5[:, WO0:WO0 + 1024],
                      woT[j * 128:(j + 1) * 128, :])
            for tb in range(4):
                tte = None
                for c in range(DC):
                    tte = mm(PS_S[:, 0:512], T5[:, WQ0 + c * 128:WQ0 + (c + 1) * 128],
                             XN[:, c * 2048 + tb * 512: c * 2048 + (tb + 1) * 512],
                             start=(c == 0), stop=(c == DC - 1),
                             waits=(("WG", twq), ("VE", C["VE"])))
                v_op(lambda e, o=T5[:, QP0 + tb * 512:QP0 + (tb + 1) * 512],
                     i=PS_S[:, 0:512]: e.tensor_copy(o, i), waits=(("TE", tte),))
            for sb in range(8):
                src = XN if sb < 4 else T3
                off = (sb % 4) * 512
                tte = None
                for c in range(DC):
                    tte = mm(PS_S[:, 512:1024], T5[:, WK0 + c * 128:WK0 + (c + 1) * 128],
                             src[:, c * 2048 + off: c * 2048 + off + 512],
                             start=(c == 0), stop=(c == DC - 1),
                             waits=(("WG", twk), ("VE", C["VE"])))
                v_op(lambda e, o=T4[:, KP0 + sb * 512:KP0 + (sb + 1) * 512],
                     i=PS_S[:, 512:1024]: e.tensor_copy(o, i), waits=(("TE", tte),))
            for st in range(32):
                src = XN if st < 16 else T3
                off = (st % 16) * 128
                tte = None
                for c in range(DC):
                    tte = mm(PS_C[:, 0:128], src[:, c * 2048 + off: c * 2048 + off + 128],
                             T5[:, WV0 + c * 128:WV0 + (c + 1) * 128],
                             start=(c == 0), stop=(c == DC - 1),
                             waits=(("WG", twv), ("VE", C["VE"])))
                v_op(lambda e, o=T4[:, VP0 + st * 128:VP0 + (st + 1) * 128],
                     i=PS_C[:, 0:128]: e.tensor_copy(o, i), waits=(("TE", tte),))
            for hp in range(2):
                r0 = hp * 64
                for tt in range(NTT):
                    tte0 = None
                    for k in range(4):
                        tte0 = mm(PS_S[:, k * 512:(k + 1) * 512],
                                  T5[r0:r0 + 64, QP0 + tt * 128:QP0 + (tt + 1) * 128],
                                  T4[r0:r0 + 64, KP0 + k * 512:KP0 + (k + 1) * 512],
                                  start=True, stop=True,
                                  waits=(("SE", C["SE"]), ("VE", C["VE"])))
                    mx0 = STAT[:, ST_A:ST_A + 1]
                    b0 = STAT[:, ST_A + 3:ST_A + 4]
                    sum0 = STAT[:, ST_A + 7:ST_A + 8]
                    v_op(lambda e, o=mx0: e.reduce_max(o, PS_S[:, :], axis=AX.X),
                         waits=(("TE", tte0),))
                    tv = v_op(lambda e, o=b0, i=mx0:
                              e.tensor_single_scalar(o, i, -0.125, ALU.mult))
                    te0 = s_op(lambda e, o=T4[:, E0:E0 + 2048], b=b0, a=sum0:
                               e.activation(o, PS_S[:, :], AF.Exp, bias=b, scale=0.125,
                                            accum_out=a), waits=(("VE", tv),))
                    tte1 = None
                    for k in range(4):
                        tte1 = mm(PS_S[:, k * 512:(k + 1) * 512],
                                  T5[r0:r0 + 64, QP0 + tt * 128:QP0 + (tt + 1) * 128],
                                  T4[r0:r0 + 64, KP0 + 2048 + k * 512:KP0 + 2048 + (k + 1) * 512],
                                  start=True, stop=True, waits=(("SE", te0),))
                    mx1 = STAT[:, ST_A + 1:ST_A + 2]
                    mrg = STAT[:, ST_A + 2:ST_A + 3]
                    b1 = STAT[:, ST_A + 4:ST_A + 5]
                    d0 = STAT[:, ST_A + 5:ST_A + 6]
                    a0 = STAT[:, ST_A + 6:ST_A + 7]
                    sum1 = STAT[:, ST_A + 8:ST_A + 9]
                    v_op(lambda e, o=mx1: e.reduce_max(o, PS_S[:, :], axis=AX.X),
                         waits=(("TE", tte1),))
                    v_op(lambda e, o=mrg, i=mx0, j_=mx1: e.tensor_tensor(o, i, j_, ALU.max))
                    v_op(lambda e, o=b1, i=mrg:
                         e.tensor_single_scalar(o, i, -0.125, ALU.mult))
                    tv = v_op(lambda e, o=d0, i=mx0, m_=mrg:
                              e.tensor_scalar(o, i, m_, 0.125, ALU.subtract, ALU.mult))
                    te1 = s_op(lambda e, o=T4[:, E0 + 2048:E0 + 4096], b=b1, a=sum1:
                               e.activation(o, PS_S[:, :], AF.Exp, bias=b, scale=0.125,
                                            accum_out=a), waits=(("VE", tv),))
                    te1 = s_op(lambda e, o=a0, i=d0: e.activation(o, i, AF.Exp),
                               waits=(("SE", te1),))
                    stot = STAT[:, ST_A + 9:ST_A + 10]
                    rs = STAT[:, ST_A + 10:ST_A + 11]
                    e0s = STAT[:, ST_A + 11:ST_A + 12]
                    v_op(lambda e, o=stot, i=sum0, a=a0, j_=sum1:
                         e.scalar_tensor_tensor(o, i, a, j_, ALU.mult, ALU.add),
                         waits=(("SE", te1),))
                    v_op(lambda e, o=rs, i=stot: e.reciprocal(o, i))
                    v_op(lambda e, o=e0s, i=a0, r=rs:
                         e.tensor_scalar(o, i, r, None, ALU.mult))
                    v_op(lambda e, o=T4[:, E0:E0 + 2048], s=e0s:
                         e.tensor_scalar(o, o, s, None, ALU.mult))
                    v_op(lambda e, o=T4[:, E0 + 2048:E0 + 4096], s=rs:
                         e.tensor_scalar(o, o, s, None, ALU.mult))
                    for b_ in range(4):
                        ps = PS_T if b_ % 2 == 0 else PS_U
                        tte = None
                        for u in range(8):
                            tte = transpose(ps[:, u * 128:(u + 1) * 128],
                                            T4[:, E0 + b_ * 1024 + u * 128:
                                               E0 + b_ * 1024 + (u + 1) * 128],
                                            waits=(("VE", C["VE"]),))
                        v_op(lambda e, o=T4[:, ET0 + b_ * 1024:ET0 + (b_ + 1) * 1024],
                             i=ps[:, :]: e.tensor_copy(o, i), waits=(("TE", tte),))
                    vread = C["VE"]
                    tte = None
                    for sc in range(32):
                        tte = mm(PS_C[0:64, 0:128],
                                 T4[:, VP0 + sc * 128 + r0: VP0 + sc * 128 + r0 + 64],
                                 T4[:, ET0 + sc * 128:ET0 + (sc + 1) * 128],
                                 start=(sc == 0), stop=(sc == 31),
                                 waits=(("VE", vread),))
                    v_op(lambda e, o=T5[r0:r0 + 64, CX0 + tt * 128:CX0 + (tt + 1) * 128],
                         i=PS_C[0:64, 0:128]: e.tensor_copy(o, i), waits=(("TE", tte),))
            for tt in range(NTT):
                for dp in range(2):
                    ps = PS_E if (tt * 2 + dp) % 2 == 0 else PS_C
                    tte = mm(ps[:, 0:512], T5[:, CX0 + tt * 128:CX0 + (tt + 1) * 128],
                             T5[:, WO0 + dp * 512:WO0 + (dp + 1) * 512],
                             start=True, stop=(j != 0),
                             waits=(("WG", two), ("VE", C["VE"])))
                    if j == 0:
                        tte = mm(ps[:, 0:512], ONES[0:1, 0:128],
                                 BIASA[0:1, dp * 512:(dp + 1) * 512],
                                 start=False, stop=True)
                    v_op(lambda e, o=X[:, tt * 1024 + dp * 512: tt * 1024 + (dp + 1) * 512],
                         i=ps[:, 0:512]: e.tensor_tensor(o, o, i, ALU.add),
                         waits=(("TE", tte),))

    CFM, MB2 = 9216, 11264

    def p2():
        tgw = dma("gpsimd", T9c[:, 0:DC * NE],
                  gate_w.rearrange("(c p) e -> p c e", p=128),
                  waits=(("VE", C["VE"]), ("TE", C["TE"])))
        tb2 = dma("gpsimd", T5[0:NE, MB2:MB2 + 1024], moe_b2,
                  waits=(("TE", C["TE"]),))
        for tt in range(NTT):
            tte = None
            for c in range(DC):
                tte = mm(PS_C[:, 0:NE],
                         XN[:, c * 2048 + tt * 128: c * 2048 + (tt + 1) * 128],
                         T9c[:, c * NE:(c + 1) * NE], start=(c == 0), stop=False,
                         waits=(("WG", tgw), ("VE", C["VE"])))
            tte = mm(PS_C[:, 0:NE], ONES[32:33, 0:128], BIASB[32:33, 0:NE],
                     start=False, stop=True)
            gs = STAT[:, 160:160 + NE]
            tmp = STAT[:, 184:184 + NE]
            tmp2 = STAT[:, 208:208 + NE]
            m1 = STAT[:, ST_G:ST_G + 1]; i1 = STAT[:, ST_G + 1:ST_G + 2]
            m2 = STAT[:, ST_G + 2:ST_G + 3]; i2 = STAT[:, ST_G + 3:ST_G + 4]
            dd = STAT[:, ST_G + 4:ST_G + 5]; ee = STAT[:, ST_G + 5:ST_G + 6]
            g1 = STAT[:, ST_G + 6:ST_G + 7]; g2 = STAT[:, ST_G + 7:ST_G + 8]
            v_op(lambda e, o=gs, i=PS_C[:, 0:NE]: e.tensor_copy(o, i),
                 waits=(("TE", tte), ("LD", tok_iota)))
            v_op(lambda e, o=m1, i=gs: e.reduce_max(o, i, axis=AX.X))
            v_op(lambda e, o=tmp, i=gs, m=m1:
                 e.tensor_single_scalar(o, i, m, ALU.is_equal))
            v_op(lambda e, o=tmp2, i=tmp: e.tensor_tensor(o, i, IOTAM[:, :], ALU.mult))
            v_op(lambda e, o=tmp2, i=tmp2: e.tensor_single_scalar(o, i, 1e4, ALU.add))
            v_op(lambda e, o=i1, i=tmp2: e.tensor_reduce(o, i, AX.X, ALU.min))
            v_op(lambda e, o=tmp, m=i1:
                 e.tensor_single_scalar(o, IOTA[:, :], m, ALU.is_equal))
            v_op(lambda e, o=tmp2, i=tmp, g=gs:
                 e.scalar_tensor_tensor(o, i, -1e30, g, ALU.mult, ALU.add))
            v_op(lambda e, o=m2, i=tmp2: e.reduce_max(o, i, axis=AX.X))
            v_op(lambda e, o=tmp2, i=tmp2, m=m2:
                 e.tensor_single_scalar(o, i, m, ALU.is_equal))
            v_op(lambda e, o=tmp2, i=tmp2: e.tensor_tensor(o, i, IOTAM[:, :], ALU.mult))
            v_op(lambda e, o=tmp2, i=tmp2: e.tensor_single_scalar(o, i, 1e4, ALU.add))
            v_op(lambda e, o=i2, i=tmp2: e.tensor_reduce(o, i, AX.X, ALU.min))
            tv = v_op(lambda e, o=dd, i=m2, m=m1:
                      e.tensor_scalar(o, i, m, None, ALU.subtract))
            ts = s_op(lambda e, o=ee, i=dd: e.activation(o, i, AF.Exp),
                      waits=(("VE", tv),))
            v_op(lambda e, o=g1, i=ee: e.tensor_single_scalar(o, i, 1.0, ALU.add),
                 waits=(("SE", ts),))
            v_op(lambda e, o=g1, i=g1: e.reciprocal(o, i))
            v_op(lambda e, o=g2, i=ee, g=g1: e.tensor_scalar(o, i, g, None, ALU.mult))
            comb = T9a[:, A_COMB + tt * NE:A_COMB + (tt + 1) * NE]
            v_op(lambda e, o=tmp, m=i1:
                 e.tensor_single_scalar(o, IOTA[:, :], m, ALU.is_equal))
            v_op(lambda e, o=comb, i=tmp, g=g1: e.tensor_scalar(o, i, g, None, ALU.mult))
            v_op(lambda e, o=tmp, m=i2:
                 e.tensor_single_scalar(o, IOTA[:, :], m, ALU.is_equal))
            v_op(lambda e, o=tmp, i=tmp, g=g2: e.tensor_scalar(o, i, g, None, ALU.mult))
            v_op(lambda e, o=comb, i=comb, j_=tmp: e.tensor_tensor(o, i, j_, ALU.add))
            cbf = T9c[:, 1024:1024 + NE]
            tv = v_op(lambda e, o=cbf, i=comb: e.tensor_copy(o, i))
            tte = transpose(PS_T[0:NE, 0:128], cbf, waits=(("VE", tv),))
            v_op(lambda e, o=T5[0:NE, CFM + tt * 128:CFM + (tt + 1) * 128],
                 i=PS_T[0:NE, 0:128]: e.tensor_copy(o, i), waits=(("TE", tte),))
        return tb2

    def p3(tb2):
        b1s = T9a[:, 512:512 + 32]
        last_w1 = C["TE"]
        last_w2 = C["TE"]
        prev_gelu = [C["SE"], C["SE"]]
        for e_ in range(NE):
            tokb1 = dma("sync", b1s, moe_b1p[:, e_ * 32:(e_ + 1) * 32],
                        waits=(("SE", C["SE"]),))
            for half in range(2):
                f0 = half * 2048
                dma("gpsimd", T3[:, 0:8192].rearrange("p (c f) -> p c f", c=4),
                    moe_w1[e_].rearrange("(c p) f -> p c f", p=128)[:, 0:4, f0:f0 + 2048],
                    waits=(("TE", last_w1),))
                w1tok = dma("gpsimd",
                            T3[:, 8192:16384].rearrange("p (c f) -> p c f", c=4),
                            moe_w1[e_].rearrange("(c p) f -> p c f", p=128)[:, 4:8, f0:f0 + 2048])
                w2tok = dma("gpsimd", T4[:, :].rearrange("p (c d) -> p c d", c=16),
                            moe_w2[e_].rearrange("(c p) d -> p c d", p=128)[:, half * 16:half * 16 + 16, :],
                            waits=(("TE", last_w2),))
                for blk in range(4):
                    gelu_last = None
                    for fc in range(16):
                        pss = PS_S[:, (fc % 2) * 512:(fc % 2) * 512 + 512]
                        tte = None
                        for c in range(DC):
                            col = (c % 4) * 2048 + fc * 128 + (8192 if c >= 4 else 0)
                            tte = mm(pss, T3[:, col:col + 128],
                                     XN[:, c * 2048 + blk * 512: c * 2048 + (blk + 1) * 512],
                                     start=(c == 0), stop=(c == DC - 1),
                                     waits=(("WG", w1tok),
                                            ("SE", prev_gelu[fc % 2]),
                                            ("VE", C["VE"])))
                        tok = s_op(
                            lambda e, o=T5[:, fc * 512:(fc + 1) * 512],
                            i=pss, b=b1s[:, half * 16 + fc: half * 16 + fc + 1]:
                            e.activation(o, i, AF.Gelu, bias=b),
                            waits=(("TE", tte), ("LD", tokb1)))
                        prev_gelu[fc % 2] = tok
                        gelu_last = tok
                    last_w1 = C["TE"]
                    for tsub in range(4):
                        tt = blk * 4 + tsub
                        for dp in range(2):
                            ps = PS_C if (tsub * 2 + dp) % 2 == 0 else PS_E
                            tte = None
                            for fc in range(16):
                                tte = mm(ps[:, 0:512],
                                         T5[:, fc * 512 + tsub * 128: fc * 512 + tsub * 128 + 128],
                                         T4[:, fc * 1024 + dp * 512: fc * 1024 + (dp + 1) * 512],
                                         start=(fc == 0), stop=(fc == 15),
                                         waits=(("SE", gelu_last), ("WG", w2tok),
                                                ("VE", C["VE"])))
                            v_op(lambda e,
                                 o=X[:, tt * 1024 + dp * 512: tt * 1024 + (dp + 1) * 512],
                                 i=ps[:, 0:512],
                                 g=T9a[:, A_COMB + tt * NE + e_:A_COMB + tt * NE + e_ + 1]:
                                 e.scalar_tensor_tensor(o, i, g, o, ALU.mult, ALU.add),
                                 waits=(("TE", tte),))
                    last_w2 = C["TE"]
        for tt in range(NTT):
            for dp in range(2):
                ps = PS_C if (tt * 2 + dp) % 2 == 0 else PS_E
                tte = mm(ps[:, 0:512], T5[0:NE, CFM + tt * 128:CFM + (tt + 1) * 128],
                         T5[0:NE, MB2 + dp * 512:MB2 + (dp + 1) * 512],
                         start=True, stop=True,
                         waits=(("WG", tb2), ("VE", C["VE"])))
                v_op(lambda e, o=X[:, tt * 1024 + dp * 512: tt * 1024 + (dp + 1) * 512],
                     i=ps[:, 0:512]: e.tensor_tensor(o, o, i, ALU.add),
                     waits=(("TE", tte),))

    def p4():
        for tt in range(NTT):
            xsl = X[:, tt * 1024:(tt + 1) * 1024]
            ln_stats(xsl, 1024, ST_M + tt, ST_R + tt, T9c[:, 1024:2048], waits=(("VE", C["VE"]),))
            nm = STAT[:, ST_B:ST_B + 1]
            mcol = STAT[:, ST_M + tt:ST_M + tt + 1]
            rcol = STAT[:, ST_R + tt:ST_R + tt + 1]
            v_op(lambda e, o=nm, i=mcol: e.tensor_single_scalar(o, i, -1.0, ALU.mult))
            xt = T9c[:, 0:1024]
            tve = v_op(lambda e, o=xt, i=xsl, a=nm, b=rcol:
                       e.tensor_scalar(o, i, a, b, ALU.add, ALU.mult))
            tte = None
            for c in range(DC):
                tte = transpose(PS_T[:, c * 128:(c + 1) * 128],
                                xt[:, c * 128:(c + 1) * 128], waits=(("VE", tve),))
            v_op(lambda e, o=xn_view(XN, tt),
                 i=PS_T[:, :].rearrange("p (c t) -> p c t", c=DC):
                 e.tensor_copy(o, i), waits=(("TE", tte),))

    def p5():
        tq1 = dma("gpsimd", T4[:, :].rearrange("p (c e) -> p c e", c=DC),
                  qkvW.rearrange("(c p) e -> p c e", p=128)[:, :, 0:2048],
                  waits=(("TE", C["TE"]), ("VE", C["VE"])))
        tq2 = dma("gpsimd", T5[:, 0:8192].rearrange("p (c e) -> p c e", c=DC),
                  qkvW.rearrange("(c p) e -> p c e", p=128)[:, :, 2048:3072],
                  waits=(("TE", C["TE"]),))
        two = dma("gpsimd", T5[:, 8192:16384].rearrange("p (c e) -> p c e", c=DC),
                  woutW.rearrange("(c p) e -> p c e", p=128),
                  waits=(("TE", C["TE"]),))
        SCL = T9a[:, 0:256]
        EX = T9a[:, 256:512]
        M16 = STAT[:, ST_A:ST_A + 16]
        S16 = STAT[:, ST_A + 16:ST_A + 32]
        scl3 = SCL.rearrange("p (h g) -> p h g", h=NH)
        ex3 = EX.rearrange("p (h g) -> p h g", h=NH)
        m3 = M16.rearrange("p (h o) -> p h o", o=1).broadcast_to([128, NH, NH])
        s3 = S16.rearrange("p (h o) -> p h o", o=1).broadcast_to([128, NH, NH])
        for tt in range(NTT):
            for ec in range(6):
                ps = PS_S[:, (ec % 4) * 512:(ec % 4) * 512 + 512]
                tte = None
                for c in range(DC):
                    if ec < 4:
                        rsrc = T4[:, c * 2048 + ec * 512: c * 2048 + (ec + 1) * 512]
                    else:
                        rsrc = T5[:, c * 1024 + (ec - 4) * 512: c * 1024 + (ec - 3) * 512]
                    tte = mm(ps, XN[:, c * 2048 + tt * 128: c * 2048 + (tt + 1) * 128],
                             rsrc, start=(c == 0), stop=False,
                             waits=(("WG", tq2 if ec >= 4 else tq1), ("VE", C["VE"])))
                if ec < 2:
                    orow, brow = ONES[64:65, 0:128], BIASA[64:65, ec * 512:(ec + 1) * 512]
                elif ec < 4:
                    orow, brow = ONES[64:65, 0:128], BIASB[64:65, (ec - 2) * 512:(ec - 1) * 512]
                else:
                    orow, brow = ONES[0:1, 0:128], BIASB[0:1, (ec - 4) * 512:(ec - 3) * 512]
                tte = mm(ps, orow, brow, start=False, stop=True)
                v_op(lambda e, o=T3[:, ec * 512:(ec + 1) * 512], i=ps:
                     e.tensor_copy(o, i), waits=(("TE", tte),))
            PROD = T9c[:, 0:1024]
            qAP = T3[:, 0:3072].rearrange("p (h w) -> p h w", h=NH)[:, :, 0:64]
            for g in range(NH):
                kg = T3[:, g * 192 + 64:g * 192 + 128]
                v_op(lambda e, o=PROD.rearrange("p (h c) -> p h c", h=NH),
                     q=qAP, k=kg.rearrange("p (o c) -> p o c", o=1).broadcast_to([128, NH, 64]):
                     e.tensor_tensor(o, q, k, ALU.mult), flush=False)
                v_op(lambda e,
                     o=SCL.rearrange("p (h g) -> p h g", h=NH)[:, :, g:g + 1].squeeze(2),
                     i=PROD.rearrange("p (h c) -> p h c", h=NH):
                     e.tensor_reduce(o, i, AX.X, ALU.add), flush=False)
            v_op(lambda e: e.tensor_copy(FLU2, FLUT[:, 2:3]))
            v_op(lambda e, o=SCL: e.tensor_single_scalar(o, o, 0.125, ALU.mult))
            v_op(lambda e, o=M16, i=scl3: e.tensor_reduce(o, i, AX.X, ALU.max))
            tv = v_op(lambda e, o=ex3, i=scl3, m=m3:
                      e.tensor_tensor(o, i, m, ALU.subtract))
            ts = s_op(lambda e, o=EX, i=EX: e.activation(o, i, AF.Exp),
                      waits=(("VE", tv),))
            v_op(lambda e, o=S16, i=ex3: e.tensor_reduce(o, i, AX.X, ALU.add),
                 waits=(("SE", ts),))
            v_op(lambda e, o=S16, i=S16: e.reciprocal(o, i))
            v_op(lambda e, o=scl3, i=ex3, r=s3: e.tensor_tensor(o, i, r, ALU.mult))
            ctx = T3[:, 3200:4224]
            for g in range(NH):
                vg = T3[:, g * 192 + 128:g * 192 + 192]
                v_op(lambda e, o=PROD.rearrange("p (h c) -> p h c", h=NH),
                     i=vg.rearrange("p (o c) -> p o c", o=1).broadcast_to([128, NH, 64]),
                     a=SCL.rearrange("p (h g) -> p h g", h=NH)[:, :, g:g + 1].broadcast_to([128, NH, 64]):
                     e.tensor_tensor(o, i, a, ALU.mult), flush=False)
                for pc, ps in ((0, PS_C), (1, PS_E)):
                    if g == 0:
                        v_op(lambda e, o=ps[:, 0:512], i=PROD[:, pc * 512:(pc + 1) * 512]:
                             e.tensor_copy(o, i), flush=False)
                    else:
                        v_op(lambda e, o=ps[:, 0:512], i=PROD[:, pc * 512:(pc + 1) * 512]:
                             e.tensor_tensor(o, o, i, ALU.add), flush=False)
            v_op(lambda e, o=ctx[:, 0:512], i=PS_C[:, 0:512]: e.tensor_copy(o, i),
                 flush=False)
            tv = v_op(lambda e, o=ctx[:, 512:1024], i=PS_E[:, 0:512]: e.tensor_copy(o, i))
            cfm = T3[:, 4224:5248]
            tte = None
            for c in range(DC):
                tte = transpose(PS_T[:, c * 128:(c + 1) * 128],
                                ctx[:, c * 128:(c + 1) * 128],
                                waits=(("VE", C["VE"]),))
            tv = v_op(lambda e, o=cfm, i=PS_T[:, :]: e.tensor_copy(o, i),
                      waits=(("TE", tte),))
            for dp in range(2):
                ps = PS_C if dp == 0 else PS_E
                tte = None
                for c in range(DC):
                    tte = mm(ps[:, 0:512], cfm[:, c * 128:(c + 1) * 128],
                             T5[:, 8192 + c * 1024 + dp * 512: 8192 + c * 1024 + (dp + 1) * 512],
                             start=(c == 0), stop=False,
                             waits=(("WG", two), ("VE", tv)))
                tte = mm(ps[:, 0:512], ONES[32:33, 0:128],
                         BIASA[32:33, dp * 512:(dp + 1) * 512], start=False, stop=True)
                v_op(lambda e, o=X[:, tt * 1024 + dp * 512: tt * 1024 + (dp + 1) * 512],
                     i=ps[:, 0:512]: e.tensor_tensor(o, o, i, ALU.add),
                     waits=(("TE", tte),))

    def w1buf(buf):
        return XN[:, buf * 4096:(buf + 1) * 4096]

    def w1slice(buf, c):
        return XN[:, buf * 4096 + c * 512: buf * 4096 + (c + 1) * 512]

    def w2buf(buf):
        return XN[:, 8192 + buf * 4096: 8192 + (buf + 1) * 4096]

    def w2slice(buf, u, dp):
        return XN[:, 8192 + buf * 4096 + u * 1024 + dp * 512:
                  8192 + buf * 4096 + u * 1024 + (dp + 1) * 512]

    def p6():
        for tt in range(NTT):
            tte = None
            for c in range(DC):
                tte = transpose(PS_T[:, c * 128:(c + 1) * 128],
                                XN[:, c * 2048 + tt * 128: c * 2048 + (tt + 1) * 128],
                                waits=(("VE", C["VE"]),))
            v_op(lambda e, o=T3[:, tt * 1024:(tt + 1) * 1024], i=PS_T[:, :]:
                 e.tensor_copy(o, i), waits=(("TE", tte),))
        trep = dma("gpsimd", T4[:, 0:15 * 1024], reps_in,
                   waits=(("TE", C["TE"]),))
        wstage = T9a[:, A_STAGE:A_STAGE + 1024]
        acc = STAT[:, ST_W:ST_W + 1]
        tmp1 = STAT[:, ST_W + 6:ST_W + 7]
        onescol = STAT[:, ST_W + 5:ST_W + 6]
        v_op(lambda e, o=onescol: e.memset(o, 1.0))
        for wi, (wsrc3, fdim, wdst, ncols, mwcol) in enumerate(
                [(l1w1T.rearrange("(c p) f -> p c f", p=128), FF, w1q_hbm, DC * FF, ST_W + 1),
                 (l1w2T.rearrange("(c p) d -> p c d", p=128), D, w2q_hbm, 32 * D, ST_W + 3)]):
            nch = ncols // 1024
            fsub = fdim // 1024

            def wsrc_chunk(ch, w3=wsrc3, fs=fsub):
                return w3[:, ch // fs, (ch % fs) * 1024:(ch % fs + 1) * 1024]
            v_op(lambda e, o=acc: e.memset(o, 0.0))
            for ch in range(nch):
                tok = dma("sync", wstage, wsrc_chunk(ch),
                          waits=(("VE", C["VE"]),))
                v_op(lambda e, o=tmp1, i=wstage:
                     e.reduce_sum(o, i, axis=AX.X, apply_absolute_value=True),
                     waits=(("LD", tok),))
                v_op(lambda e, o=acc, i=acc, j_=tmp1: e.tensor_tensor(o, i, j_, ALU.add))
            tte = mm(PS_C[0:1, 0:1], acc, onescol, start=True, stop=True,
                     waits=(("VE", C["VE"]),))
            mean1 = STAT[0:1, ST_W + 8:ST_W + 9]
            v_op(lambda e, o=mean1, i=PS_C[0:1, 0:1]:
                 e.tensor_single_scalar(o, i, 1.0 / (D * FF), ALU.mult),
                 waits=(("TE", tte),))
            tv = v_op(lambda e, o=mean1, i=mean1:
                      e.tensor_single_scalar(o, i, 1e-5, ALU.max))
            tte = mm(PS_C[:, 0:1], ONESF[:, :], mean1, start=True, stop=True,
                     waits=(("VE", tv),))
            mw = STAT[:, mwcol:mwcol + 1]
            rmw = STAT[:, mwcol + 1:mwcol + 2]
            v_op(lambda e, o=mw, i=PS_C[:, 0:1]: e.tensor_copy(o, i),
                 waits=(("TE", tte),))
            v_op(lambda e, o=rmw, i=mw: e.reciprocal(o, i))
            qt = T9c[:, 0:1024]
            for ch in range(nch):
                tok = dma("sync", wstage, wsrc_chunk(ch),
                          waits=(("VE", C["VE"]),))
                v_op(lambda e, o=wstage, i=wstage, r=rmw:
                     e.tensor_scalar(o, i, r, MAGIC, ALU.mult, ALU.add),
                     waits=(("LD", tok),))
                v_op(lambda e, o=wstage, i=wstage:
                     e.tensor_single_scalar(o, i, -MAGIC, ALU.add))
                v_op(lambda e, o=wstage, i=wstage:
                     e.tensor_single_scalar(o, i, 1.0, ALU.min))
                tv = v_op(lambda e, o=qt, i=wstage:
                          e.tensor_single_scalar(o, i, -1.0, ALU.max),
                          waits=(("LD", C["LD"]),))
                dma("sync", wdst[:, ch * 1024:(ch + 1) * 1024], qt,
                    waits=(("VE", tv),))
        wq_last = [C["TE"], C["TE"]]
        w2_last = [C["TE"], C["TE"]]
        for pb in range(8):
            cm1 = {}
            cm2 = {}
            for tp in range(2):
                tt = pb * 2 + tp
                xsl = T3[:, tt * 1024:(tt + 1) * 1024]
                ln_stats(xsl, 1024, ST_M, ST_R, T9c[:, tp * 1024:(tp + 1) * 1024], waits=(("WG", trep),))
                nm = STAT[:, ST_B:ST_B + 1]
                y1 = T9a[:, A_STAGE:A_STAGE + 1024]
                v_op(lambda e, o=nm, i=STAT[:, ST_M:ST_M + 1]:
                     e.tensor_single_scalar(o, i, -1.0, ALU.mult))
                v_op(lambda e, o=y1, i=xsl, a=nm, b=STAT[:, ST_R:ST_R + 1]:
                     e.tensor_scalar(o, i, a, b, ALU.add, ALU.mult))
                v_op(lambda e, o=y1, i=y1, g=T4[:, 0:1024]:
                     e.tensor_tensor(o, i, g, ALU.mult))
                v_op(lambda e, o=y1, i=y1, b=T4[:, 1024:2048]:
                     e.tensor_tensor(o, i, b, ALU.add))
                am = STAT[:, ST_Q:ST_Q + 1]
                sx = STAT[:, ST_Q + 1:ST_Q + 2]
                cmc = STAT[:, ST_Q + 2 + tp:ST_Q + 3 + tp]
                v_op(lambda e, o=am, i=y1:
                     e.reduce_max(o, i, axis=AX.X, apply_absolute_value=True))
                v_op(lambda e, o=am, i=am: e.tensor_single_scalar(o, i, 1e-5, ALU.max))
                v_op(lambda e, o=sx, i=am: e.reciprocal(o, i))
                v_op(lambda e, o=sx, i=sx:
                     e.tensor_single_scalar(o, i, 127.0, ALU.mult))
                v_op(lambda e, o=cmc, i=am:
                     e.tensor_single_scalar(o, i, 1.0 / 127.0, ALU.mult))
                v_op(lambda e, o=cmc, i=cmc, m=STAT[:, ST_W + 1:ST_W + 2]:
                     e.tensor_scalar(o, i, m, None, ALU.mult))
                cm1[tp] = cmc
                qx = T9c[:, tp * 1024:(tp + 1) * 1024]
                v_op(lambda e, o=y1, i=y1, s=sx:
                     e.tensor_scalar(o, i, s, MAGIC, ALU.mult, ALU.add))
                v_op(lambda e, o=y1, i=y1: e.tensor_single_scalar(o, i, -MAGIC, ALU.add))
                v_op(lambda e, o=y1, i=y1: e.tensor_single_scalar(o, i, 127.0, ALU.min))
                tve = v_op(lambda e, o=qx, i=y1:
                           e.tensor_single_scalar(o, i, -128.0, ALU.max))
                tte = None
                for c in range(DC):
                    tte = transpose(PS_U[:, c * 128:(c + 1) * 128],
                                    qx[:, c * 128:(c + 1) * 128],
                                    waits=(("VE", C["VE"]),))
                tvq = v_op(lambda e, o=qx, i=PS_U[:, :]: e.tensor_copy(o, i),
                           waits=(("TE", tte),))
                if DEBUG_TAP and pb == 0 and tp == 0:
                    dma("sync", dbg_y1, y1, waits=(("VE", tvq),))
                    dma("sync", dbg_qx, qx, waits=(("VE", tvq),))
            for fg in range(8):
                buf = fg % 2
                wtok = dma("sync",
                           w1buf(buf).rearrange("p (c f) -> p c f", c=DC),
                           w1q_hbm[:, :].rearrange("p (c f) -> p c f", c=DC)[:, :, fg * 512:(fg + 1) * 512],
                           waits=(("TE", wq_last[buf]),))
                for tp in range(2):
                    ps = PS_C if tp == 0 else PS_E
                    qx = T9c[:, tp * 1024:(tp + 1) * 1024]
                    tte = None
                    for c in range(DC):
                        tte = mm(ps[:, 0:512], qx[:, c * 128:(c + 1) * 128],
                                 w1slice(buf, c), start=(c == 0), stop=(c == DC - 1),
                                 waits=(("LD", wtok), ("VE", C["VE"])))
                    v_op(lambda e,
                         o=T5[:, tp * 8192 + fg * 512: tp * 8192 + (fg + 1) * 512],
                         i=ps[:, 0:512], s=cm1[tp],
                         b=T4[:, 10240 + fg * 512:10240 + (fg + 1) * 512]:
                         e.scalar_tensor_tensor(o, i, s, b, ALU.mult, ALU.add),
                         waits=(("TE", tte),))
                wq_last[buf] = C["TE"]
            for tp in range(2):
                tt = pb * 2 + tp
                H = T5[:, tp * 8192:tp * 8192 + 4096]
                Hg = T5[:, tp * 8192 + 4096:tp * 8192 + 8192]
                if DEBUG_TAP and pb == 0 and tp == 0:
                    dma("sync", dbg_h, H, waits=(("VE", C["VE"]),))
                ts = s_op(lambda e, o=Hg, i=H: e.activation(o, i, AF.Gelu),
                          waits=(("VE", C["VE"]),))
                if DEBUG_TAP and pb == 0 and tp == 0:
                    dma("sync", dbg_hg, Hg, waits=(("SE", ts),))
                ln_stats(Hg, 4096, ST_M, ST_R, H, waits=(("SE", ts),))
                nm = STAT[:, ST_B:ST_B + 1]
                v_op(lambda e, o=nm, i=STAT[:, ST_M:ST_M + 1]:
                     e.tensor_single_scalar(o, i, -1.0, ALU.mult))
                v_op(lambda e, o=H, i=Hg, a=nm, b=STAT[:, ST_R:ST_R + 1]:
                     e.tensor_scalar(o, i, a, b, ALU.add, ALU.mult))
                v_op(lambda e, o=H, i=H, g=T4[:, 2048:6144]:
                     e.tensor_tensor(o, i, g, ALU.mult))
                v_op(lambda e, o=H, i=H, b=T4[:, 6144:10240]:
                     e.tensor_tensor(o, i, b, ALU.add))
                am = STAT[:, ST_Q:ST_Q + 1]
                sx = STAT[:, ST_Q + 1:ST_Q + 2]
                cmc = STAT[:, ST_Q + 4 + tp:ST_Q + 5 + tp]
                v_op(lambda e, o=am, i=H:
                     e.reduce_max(o, i, axis=AX.X, apply_absolute_value=True))
                v_op(lambda e, o=am, i=am: e.tensor_single_scalar(o, i, 1e-5, ALU.max))
                v_op(lambda e, o=sx, i=am: e.reciprocal(o, i))
                v_op(lambda e, o=sx, i=sx:
                     e.tensor_single_scalar(o, i, 127.0, ALU.mult))
                v_op(lambda e, o=cmc, i=am:
                     e.tensor_single_scalar(o, i, 1.0 / 127.0, ALU.mult))
                v_op(lambda e, o=cmc, i=cmc, m=STAT[:, ST_W + 3:ST_W + 4]:
                     e.tensor_scalar(o, i, m, None, ALU.mult))
                cm2[tp] = cmc
                stg = T9a[:, A_STAGE:A_STAGE + 1024]
                for ch in range(4):
                    hs = H[:, ch * 1024:(ch + 1) * 1024]
                    v_op(lambda e, o=stg, i=hs, s=sx:
                         e.tensor_scalar(o, i, s, MAGIC, ALU.mult, ALU.add),
                         flush=False)
                    v_op(lambda e, o=stg, i=stg:
                         e.tensor_single_scalar(o, i, -MAGIC, ALU.add), flush=False)
                    v_op(lambda e, o=stg, i=stg:
                         e.tensor_single_scalar(o, i, 127.0, ALU.min), flush=False)
                    v_op(lambda e, o=Hg[:, ch * 1024:(ch + 1) * 1024], i=stg:
                         e.tensor_single_scalar(o, i, -128.0, ALU.max), flush=False)
                v_op(lambda e: e.tensor_copy(FLU2, FLUT[:, 2:3]))
                for b_ in range(4):
                    pst = PS_T if b_ % 2 == 0 else PS_U
                    tte = None
                    for u in range(8):
                        idx = b_ * 8 + u
                        tte = transpose(pst[:, u * 128:(u + 1) * 128],
                                        Hg[:, idx * 128:(idx + 1) * 128],
                                        waits=(("VE", C["VE"]),))
                    v_op(lambda e, o=Hg[:, b_ * 1024:(b_ + 1) * 1024], i=pst[:, :]:
                         e.tensor_copy(o, i), waits=(("TE", tte),))
            groups = {(0, 0): PS_S[:, 0:512], (0, 1): PS_S[:, 512:1024],
                      (1, 0): PS_C[:, 0:512], (1, 1): PS_E[:, 0:512]}
            evac_gate = C["VE"]
            for fcg in range(8):
                buf = fcg % 2
                wtok = dma("sync", w2buf(buf),
                           w2q_hbm[:, fcg * 4096:(fcg + 1) * 4096],
                           waits=(("TE", w2_last[buf]),))
                for tp in range(2):
                    for dp in range(2):
                        tte = None
                        for u in range(4):
                            fc = fcg * 4 + u
                            tte = mm(groups[(tp, dp)],
                                     T5[:, tp * 8192 + 4096 + fc * 128:
                                        tp * 8192 + 4096 + (fc + 1) * 128],
                                     w2slice(buf, u, dp),
                                     start=(fc == 0), stop=(fc == 31),
                                     waits=(("LD", wtok), ("VE", evac_gate)))
                w2_last[buf] = C["TE"]
            mm2_done = C["TE"]
            if DEBUG_TAP and pb == 0:
                dma("sync", dbg_qh, T5[:, 4096:8192], waits=(("TE", mm2_done),))
            for tp in range(2):
                tt = pb * 2 + tp
                for dp in range(2):
                    v_op(lambda e,
                         o=X[:, tt * 1024 + dp * 512: tt * 1024 + (dp + 1) * 512],
                         i=groups[(tp, dp)], s=cm2[tp]:
                         e.scalar_tensor_tensor(o, i, s, o, ALU.mult, ALU.add),
                         waits=(("TE", mm2_done),))
                tvx = v_op(lambda e, o=X[:, tt * 1024:(tt + 1) * 1024],
                           i=T4[:, 14336:15360]: e.tensor_tensor(o, o, i, ALU.add))
            if DEBUG_TAP and pb == 0:
                dma("sync", dbg_x01, X[:, 0:2048], waits=(("VE", tvx),))

    with nc.Block() as blk:
        p0()
        tb2 = None
        if STOP_AFTER >= 1:
            p1()
        if STOP_AFTER >= 2:
            tb2 = p2()
        if STOP_AFTER >= 3:
            p3(tb2)
        if STOP_AFTER >= 4:
            p4()
        if STOP_AFTER >= 5:
            p5()
        if STOP_AFTER >= 6:
            p6()
        if DEBUG_TAP:
            dma("sync", dbg_xn, XN[:, :],
                waits=(("VE", C["VE"]), ("SE", C["SE"]), ("TE", C["TE"])))
            dma("sync", dbg_t3, T3[:, :])
            dma("sync", dbg_t5, T5[:, :])
            dma("sync", dbg_t4, T4[:, :])
            dma("sync", dbg_st, STAT[:, :])
        dma("sync", y.rearrange("(tt p) d -> p tt d", p=128), X[:, :],
            waits=(("VE", C["VE"]), ("SE", C["SE"]), ("TE", C["TE"]),
                   ("WG", C["WG"])))
        final_ld = C["LD"]

        def make_body(name):
            def body(eng):
                last = {}
                for (en, ws, emit, inc, k) in items:
                    if en != name:
                        continue
                    for (sem, val) in ws:
                        if OWNER[sem] == name:
                            continue
                        if val > last.get(sem, 0):
                            eng.wait_ge(sems[sem], val)
                            last[sem] = val
                    ins = emit(eng)
                    if inc is not None:
                        ins.then_inc(sems[inc], k)
                if name == "sync":
                    eng.wait_ge(sems["LD"], final_ld)
            return body

        blk.sync(make_body("sync"))
        blk.gpsimd(make_body("gpsimd"))
        blk.tensor(make_body("tensor"))
        blk.vector(make_body("vector"))
        blk.scalar(make_body("scalar"))

    for m in reversed(mgr):
        m.__exit__(None, None, None)
    return nc


def _prep_shared(inputs):
    f32 = np.float32
    d = {}
    d["wqT"] = np.ascontiguousarray(np.asarray(inputs["L0_attn_wq"], f32).T)
    d["wkT"] = np.ascontiguousarray(np.asarray(inputs["L0_attn_wk"], f32).T)
    d["wvT"] = np.ascontiguousarray(np.asarray(inputs["L0_attn_wv"], f32).T)
    d["woT"] = np.ascontiguousarray(np.asarray(inputs["L0_attn_wo"], f32).T)
    d["gate_w"] = np.ascontiguousarray(np.asarray(inputs["L0_ffn_gate_w"], f32))
    d["moe_w1"] = np.ascontiguousarray(np.asarray(inputs["L0_ffn_w1"], f32))
    d["moe_w2"] = np.ascontiguousarray(np.asarray(inputs["L0_ffn_w2"], f32))
    b1 = np.asarray(inputs["L0_ffn_b1"], f32)
    d["moe_b1p"] = np.ascontiguousarray(
        b1.reshape(NE, 32, 128).transpose(2, 0, 1).reshape(128, NE * 32))
    d["moe_b2"] = np.ascontiguousarray(np.asarray(inputs["L0_ffn_b2"], f32))
    d["qkvW"] = np.ascontiguousarray(np.asarray(inputs["L1_attn_W_qkv"], f32))
    d["woutW"] = np.ascontiguousarray(np.asarray(inputs["L1_attn_W_out"], f32))
    d["l1w1T"] = np.ascontiguousarray(np.asarray(inputs["L1_ffn_w1"], f32).T)
    d["l1w2T"] = np.ascontiguousarray(np.asarray(inputs["L1_ffn_w2"], f32).T)
    biasA = np.zeros((128, 1024), f32)
    biasA[0] = np.asarray(inputs["L0_attn_bo"], f32)
    biasA[32] = np.asarray(inputs["L1_attn_b_out"], f32)
    bqkv = np.asarray(inputs["L1_attn_b_qkv"], f32)
    biasA[64] = bqkv[0:1024]
    biasB = np.zeros((128, 1024), f32)
    biasB[64] = bqkv[1024:2048]
    biasB[0] = bqkv[2048:3072]
    biasB[32, 0:NE] = np.asarray(inputs["L0_ffn_gate_b"], f32)
    d["biasA_in"] = biasA
    d["biasB_in"] = biasB
    reps = np.zeros((128, 15 * 1024), f32)
    reps[:, 0:1024] = np.asarray(inputs["L1_ffn_ln1_g"], f32)
    reps[:, 1024:2048] = np.asarray(inputs["L1_ffn_ln1_b"], f32)
    reps[:, 2048:6144] = np.asarray(inputs["L1_ffn_ln2_g"], f32)
    reps[:, 6144:10240] = np.asarray(inputs["L1_ffn_ln2_b"], f32)
    reps[:, 10240:14336] = np.asarray(inputs["L1_ffn_b1"], f32)
    reps[:, 14336:15360] = np.asarray(inputs["L1_ffn_b2"], f32)
    d["reps_in"] = reps
    d["idn_in"] = np.eye(128, dtype=f32)
    d["iota_in"] = np.broadcast_to(np.arange(NE, dtype=f32), (128, NE)).copy()
    d["iotam_in"] = (d["iota_in"] - 1e4).astype(f32)
    return d


def kernel(**inputs):
    shared = _prep_shared(inputs)
    x = np.ascontiguousarray(np.asarray(inputs["L0_attn_x"], np.float32))
    nc = build_nc()
    in_maps = []
    for c in range(8):
        b, hf = c // 2, c % 2
        im = dict(shared)
        im["x_own"] = np.ascontiguousarray(x[b, hf * 2048:(hf + 1) * 2048])
        im["x_oth"] = np.ascontiguousarray(x[b, (1 - hf) * 2048:(2 - hf) * 2048])
        in_maps.append(im)
    res = run_bass_kernel_spmd(nc, in_maps, core_ids=list(range(8)))
    global LAST_RES
    LAST_RES = res
    out = np.empty((4, 4096, 1024), np.float32)
    for c in range(8):
        b, hf = c // 2, c % 2
        out[b, hf * 2048:(hf + 1) * 2048] = res.results[c]["y"]
    return out
